# version 05
# baseline (speedup 1.0000x reference)
import numpy as np
import ml_dtypes
import concourse.bass as bass
import concourse.mybir as mybir
from concourse.bass_utils import run_bass_kernel_spmd

F32 = mybir.dt.float32
BF16 = mybir.dt.bfloat16
I32 = mybir.dt.int32
AF = mybir.ActivationFunctionType
ALU = mybir.AluOpType
AX = mybir.AxisListType

D = 4096
B = 4
S = 4096
H = 32
NT = 32
R = 2048
NRT = 16
NCL = 64
EPS = 1e-5
ISQ = float(1.0 / np.sqrt(128.0))
NCORES = 8

bA, bB, bC, bD_, bE, bF = 0, 4096, 8192, 12288, 16384, 20480


class _B:
    ENGS = ("te", "ve", "sc", "gp", "sy")

    def __init__(self):
        self.q = {e: [] for e in self.ENGS}
        self.cnt = {}
        self.owner = {}
        for s, o in (("sLD", None), ("sCA", None), ("sST", None), ("sCC", None),
                     ("sTE", "te"), ("sVE", "ve"), ("sSC", "sc"), ("sGP", "gp")):
            self.cnt[s] = 0
            self.owner[s] = o

    def add(self, eng, fn, waits=(), inc=None, amt=None):
        w = tuple(t for t in waits if t is not None)
        if amt is None:
            amt = 16 if inc in ("sLD", "sCA", "sST") else 1
        self.q[eng].append((w, fn, inc, amt))
        if inc is not None:
            self.cnt[inc] += amt
            return (inc, self.cnt[inc])
        return None

    def jall(self):
        return [(s, c) for s, c in self.cnt.items() if c > 0]


_DBG = []
_STOP = ""
_DOTS_RT = NRT


def _build():
    nc = bass.Bass("TRN2", target_bir_lowering=False, debug=False)

    x0T = nc.dram_tensor("x0T", [D, R], F32, kind="ExternalInput").ap()
    wq_t = nc.dram_tensor("wq_t", [NT, 128, NT, 128], F32, kind="ExternalInput").ap()
    wk_t = nc.dram_tensor("wk_t", [NT, 128, NT, 128], F32, kind="ExternalInput").ap()
    wv_t = nc.dram_tensor("wv_t", [8, 128, NT, 512], F32, kind="ExternalInput").ap()
    wo_t = nc.dram_tensor("wo_t", [NT, 128, NT, 128], F32, kind="ExternalInput").ap()
    wf_t = nc.dram_tensor("wf_t", [NT, 128, NT, 128], F32, kind="ExternalInput").ap()
    ct_t = nc.dram_tensor("ct_t", [128, NT, NCL], F32, kind="ExternalInput").ap()
    bias_t = nc.dram_tensor("bias_t", [128, NT], F32, kind="ExternalInput").ap()
    cos_t = nc.dram_tensor("cos_t", [128, R], F32, kind="ExternalInput").ap()
    sin_t = nc.dram_tensor("sin_t", [128, R], F32, kind="ExternalInput").ap()
    mask_d = nc.dram_tensor("mask_d", [128, 256], BF16, kind="ExternalInput").ap()
    asn_t = nc.dram_tensor("asn_t", [1, D], I32, kind="ExternalInput").ap()
    iden_t = nc.dram_tensor("iden_t", [128, 128], F32, kind="ExternalInput").ap()
    outT = nc.dram_tensor("outT", [D, R], F32, kind="ExternalOutput").ap()

    x1T = nc.dram_tensor("x1T", [D, R], F32).ap()
    x2T = nc.dram_tensor("x2T", [D, R], F32).ap()
    qTd = nc.dram_tensor("qTd", [D, R], BF16).ap()
    kTd = nc.dram_tensor("kTd", [D, R], BF16).ap()
    vd = [nc.dram_tensor(f"vd{g}", [R, 1024], BF16).ap() for g in range(4)]
    kg = nc.dram_tensor("kg", [2 * D, R], BF16).ap()
    vg = [nc.dram_tensor(f"vg{g}", [2 * R, 1024], BF16).ap() for g in range(4)]
    rowd = nc.dram_tensor("rowd", [1, 8192], F32).ap()
    asnf = nc.dram_tensor("asnf", [1, D], F32).ap()
    cc_in = [nc.dram_tensor(f"cc_in{i}", [1, NCL], F32).ap() for i in range(2)]
    cc_out = [nc.dram_tensor(f"cc_out{i}", [1, NCL], F32).ap() for i in range(2)]

    bld = _B()
    A = bld.add

    from contextlib import ExitStack
    ctx = ExitStack()
    en = ctx.enter_context

    big = en(nc.sbuf_tensor("big", [128, NT * R], BF16))
    bfu = en(nc.sbuf_tensor("bfu", [128, 21504], BF16))
    f32u = en(nc.sbuf_tensor("f32u", [128, 8192], F32))
    mask2 = en(nc.sbuf_tensor("mask2", [128, 256], BF16))
    iden = en(nc.sbuf_tensor("iden", [128, 128], F32))
    ones_bf = en(nc.sbuf_tensor("ones_bf", [128, 1], BF16))
    ones_f = en(nc.sbuf_tensor("ones_f", [128, 1], F32))
    ones_r = en(nc.sbuf_tensor("ones_r", [1, 128], F32))
    io64 = en(nc.sbuf_tensor("io64", [64, 1], F32))
    bias_s = en(nc.sbuf_tensor("bias_s", [128, NT], F32))
    rsel = en(nc.sbuf_tensor("rsel", [128, NT], F32))
    brsel = en(nc.sbuf_tensor("brsel", [128, NT], F32))
    asn_i = f32u[0:1, 4096:8192].bitcast(I32)
    asn_f = f32u[0:1, 0:4096]
    clrow = en(nc.sbuf_tensor("clrow", [1, NCL], F32))
    clcol = en(nc.sbuf_tensor("clcol", [64, 1], BF16))
    pb = [en(nc.psum_tensor(f"pb{i}", [128, 512], F32)) for i in range(8)]

    handles = {}
    for s in bld.cnt:
        handles[s] = en(nc.semaphore(s))

    t_mask = A("sy", lambda e: e.dma_start(out=mask2[:, :], in_=mask_d), inc="sLD")
    t_iden = A("sy", lambda e: e.dma_start(out=iden[:, :], in_=iden_t), inc="sLD")
    t_bias = A("sy", lambda e: e.dma_start(out=bias_s[:, :], in_=bias_t), inc="sLD")
    t_asn = A("sy", lambda e: e.dma_start(out=asn_i[:, :], in_=asn_t), inc="sLD")
    t_ct = A("gp", lambda e: e.dma_start(
        out=bfu[:, bB:bB + 2048], in_=ct_t.rearrange("p t c -> p (t c)")), inc="sCA")
    A("ve", lambda e: e.memset(ones_bf[:, :], 1.0), inc="sVE")
    t_ones = A("ve", lambda e: e.memset(ones_f[:, :], 1.0), inc="sVE")
    t_onesr = A("ve", lambda e: e.memset(ones_r[:, :], 1.0), inc="sVE")
    t_io = A("gp", lambda e: e.iota(io64[:, :], pattern=[[0, 1]], base=0,
                                    channel_multiplier=1,
                                    allow_small_or_imprecise_dtypes=True), inc="sGP")
    t_af = A("ve", lambda e: e.tensor_copy(asn_f[:, :], asn_i[:, :]),
             waits=[t_asn], inc="sVE")
    t_afd = A("sc", lambda e: e.dma_start(out=asnf, in_=asn_f[:, :]),
              waits=[t_af], inc="sST")


    def build_onehot(pw):
        asnb = f32u[0:64, 1024:5120]
        oh = bfu[0:64, bC:bC + 4096]
        a1 = A("sy", lambda e: e.dma_start(out=asnb[:, :],
                                           in_=asnf.partition_broadcast(64)),
               waits=pw + [t_afd], inc="sLD")
        a2 = A("ve", lambda e: e.tensor_scalar(oh[:, :], asnb[:, :], io64[:, :],
                                               None, ALU.is_equal),
               waits=pw + [a1, t_io], inc="sVE")
        return a2

    def ln_pass(src, src_toks, pw):
        xld = [f32u[:, 0:2048], f32u[:, 2048:4096]]
        xb = bfu[:, bA:bA + 2048]
        xsq = bfu[:, bA + 2048:bA + 4096]
        t_cast = [None, None]
        t_mm_last = None
        t_sq = None
        for t in range(NT):
            s = t % 2
            tl = A("sy", lambda e, t=t, s=s: e.dma_start(
                out=xld[s][:, :], in_=src[t * 128:(t + 1) * 128, :]),
                waits=pw + [t_cast[s], src_toks[t] if src_toks else None], inc="sLD")
            tc = A("ve", lambda e, s=s: e.tensor_copy(xb[:, :], xld[s][:, :]),
                   waits=pw + [tl, t_mm_last], inc="sVE")
            t_cast[s] = tc
            t_sq = A("ve", lambda e: e.tensor_mul(xsq[:, :], xb[:, :], xb[:, :]),
                     waits=[tc], inc="sVE")
            for rc in range(4):
                bk = pb[4 + rc]
                A("te", lambda e, bk=bk, rc=rc, t=t: e.matmul(
                    bk[0:1, :], ones_bf[:, :], xb[:, rc * 512:(rc + 1) * 512],
                    start=(t == 0), stop=False, skip_group_check=True),
                    waits=pw + [tc])
                tm = A("te", lambda e, bk=bk, rc=rc, t=t: e.matmul(
                    bk[32:33, :], ones_bf[:, :], xsq[:, rc * 512:(rc + 1) * 512],
                    start=(t == 0), stop=(t == NT - 1), skip_group_check=True),
                    waits=[t_sq], inc="sTE" if (rc == 3) else None)
                if rc == 3:
                    t_mm_last = tm
        mrow = f32u[0:1, 4096:6144]
        rrow = f32u[0:1, 6144:8192]
        tmpr = f32u[0:1, 0:2048]
        tok_ev = []
        for rc in range(4):
            bk = pb[4 + rc]
            sl = slice(rc * 512, (rc + 1) * 512)
            A("ve", lambda e, bk=bk, sl=sl: e.tensor_scalar_mul(
                mrow[:, sl], bk[0:1, 0:512], 1.0 / D), waits=[t_mm_last], inc="sVE")
            e2 = A("ve", lambda e, bk=bk, sl=sl: e.tensor_scalar_mul(
                rrow[:, sl], bk[32:33, 0:512], 1.0 / D), waits=[t_mm_last], inc="sVE")
            tok_ev.append(e2)
        t1 = A("ve", lambda e: e.tensor_mul(tmpr[:, :], mrow[:, :], mrow[:, :]),
               waits=tok_ev, inc="sVE")
        t2 = A("ve", lambda e: e.tensor_sub(rrow[:, :], rrow[:, :], tmpr[:, :]),
               waits=[t1], inc="sVE")
        t2b = A("ve", lambda e: e.tensor_scalar_add(rrow[:, :], rrow[:, :], EPS),
                waits=[t2], inc="sVE")
        t3 = A("sc", lambda e: e.activation(tmpr[:, :], rrow[:, :], AF.Ln),
               waits=[t2b], inc="sSC")
        t4 = A("sc", lambda e: e.activation(rrow[:, :], tmpr[:, :], AF.Exp,
                                            bias=0.0, scale=-0.5),
               waits=[t3], inc="sSC")
        meanb = f32u[:, 4096:6144]
        rstdb = f32u[:, 6144:8192]
        b1 = b2 = None
        t_bf = [tok_ev[-1], t4]
        for rc in range(4):
            sl = slice(rc * 512, (rc + 1) * 512)
            mb = A("te", lambda e, sl=sl: e.matmul(
                pb[4][:, 0:512], ones_r[:, :], mrow[:, sl],
                start=True, stop=True, skip_group_check=True),
                waits=[tok_ev[-1], t1, t_onesr, t_bf[0]], inc="sTE")
            b1 = A("ve", lambda e, sl=sl: e.tensor_copy(meanb[:, sl],
                                                        pb[4][:, 0:512]),
                   waits=[mb], inc="sVE")
            t_bf[0] = b1
            rb = A("te", lambda e, sl=sl: e.matmul(
                pb[5][:, 0:512], ones_r[:, :], rrow[:, sl],
                start=True, stop=True, skip_group_check=True),
                waits=[t4, t_onesr, t_bf[1]], inc="sTE")
            b2 = A("ve", lambda e, sl=sl: e.tensor_copy(rstdb[:, sl],
                                                        pb[5][:, 0:512]),
                   waits=[rb], inc="sVE")
            t_bf[1] = b2
        xrd = [f32u[:, 0:2048], f32u[:, 2048:4096]]
        t_n2 = [t_cast[0], t_cast[1]]
        tok_h = [None] * NT
        for t in range(NT):
            s = t % 2
            tl = A("sy", lambda e, t=t, s=s: e.dma_start(
                out=xrd[s][:, :], in_=src[t * 128:(t + 1) * 128, :]),
                waits=[t_n2[s], t4 if t < 2 else None], inc="sLD")
            n1 = A("ve", lambda e, s=s: e.tensor_sub(xrd[s][:, :], xrd[s][:, :],
                                                     meanb[:, :]),
                   waits=[tl, b1], inc="sVE")
            n2 = A("ve", lambda e, s=s, t=t: e.tensor_mul(
                big[:, t * R:(t + 1) * R], xrd[s][:, :], rstdb[:, :]),
                waits=[n1, b2], inc="sVE")
            t_n2[s] = n2
            tok_h[t] = n2
        return tok_h

    def dots_sels(tok_h, cc_idx, pw):
        ctb = bfu[:, bB:bB + 2048]
        hits = f32u[:, 0:64]
        expd = f32u[:, 64:128]
        negm = f32u[:, 128:129]
        scol = f32u[:, 130:131]
        thr = f32u[:, 132:133]
        qcnt = f32u[:, 192:208]
        qsel01 = f32u[:, 208:224]
        qtr = f32u[0:16, 224:352]
        cntrow = f32u[0:1, 6144:6208]
        t_ex = [None, None]
        t_ex_last = None
        t_h = None
        t_cnt = None
        for rt in range(_DOTS_RT):
            bk = pb[2 + rt % 2]
            mm = None
            for t in range(NT):
                mm = A("te", lambda e, bk=bk, t=t, rt=rt: e.matmul(
                    bk[:, 0:NCL], big[:, t * R + rt * 128: t * R + rt * 128 + 128],
                    ctb[:, t * NCL:(t + 1) * NCL],
                    start=(t == 0), stop=(t == NT - 1), skip_group_check=True),
                    waits=pw + [tok_h[t], t_ct if (t == 0 and rt == 0) else None,
                                t_ex[rt % 2] if t == 0 else None],
                    inc="sTE" if t == NT - 1 else None)
            nm = A("ve", lambda e, bk=bk: e.reduce_max(
                negm[:, :], bk[:, 0:NCL], axis=AX.X, negate=True),
                waits=[mm, t_ex_last], inc="sVE")
            nm10 = A("ve", lambda e: e.tensor_scalar_mul(negm[:, :], negm[:, :],
                                                         10.0),
                     waits=[nm], inc="sVE")
            ex = A("sc", lambda e, bk=bk: e.activation(
                expd[:, :], bk[:, 0:NCL], AF.Exp, bias=negm[:, :], scale=10.0),
                waits=pw + [nm10, t_h], inc="sSC")
            t_ex[rt % 2] = ex
            t_ex_last = ex
            sm = A("ve", lambda e: e.reduce_sum(scol[:, :], expd[:, :], axis=AX.X),
                   waits=[ex], inc="sVE")
            th = A("ve", lambda e: e.tensor_scalar_mul(thr[:, :], scol[:, :], 0.01),
                   waits=[sm], inc="sVE")
            ht = A("ve", lambda e: e.tensor_scalar(
                hits[:, :], expd[:, :], thr[:, :], None, ALU.is_ge),
                waits=[t_cnt, th], inc="sVE")
            t_cnt = A("te", lambda e, rt=rt: e.matmul(
                pb[4][0:1, 0:NCL], ones_f[:, :], hits[:, :],
                start=(rt == 0), stop=(rt == _DOTS_RT - 1), skip_group_check=True),
                waits=pw + [ht, t_ones], inc="sTE")
            t_h = A("ve", lambda e, rt=rt: e.reduce_sum(
                qcnt[:, rt:rt + 1], hits[:, :], axis=AX.X), waits=[ht], inc="sVE")
        if _DOTS_RT != NRT:
            A("ve", lambda e: e.tensor_copy(f32u[:, 352:416], pb[2][:, 0:NCL]),
              waits=[t_cnt, t_ex_last], inc="sVE")
        cv = A("ve", lambda e: e.tensor_copy(cntrow[:, :], pb[4][0:1, 0:NCL]),
               waits=[t_cnt], inc="sVE")
        cd = A("sc", lambda e: e.dma_start(out=cc_in[cc_idx], in_=cntrow[:, :]),
               waits=[cv], inc="sST")
        cc = A("gp", lambda e: e.collective_compute(
            "AllReduce", ALU.add, replica_groups=[list(range(NCORES))],
            ins=[cc_in[cc_idx].opt()], outs=[cc_out[cc_idx].opt()]),
            waits=pw + [cd], inc="sCC", amt=1)
        cl = A("sy", lambda e: e.dma_start(out=clrow[:, :], in_=cc_out[cc_idx]),
               waits=[cc], inc="sLD")
        cs = A("ve", lambda e: e.tensor_scalar(clrow[:, :], clrow[:, :], 0.5,
                                               None, ALU.is_ge),
               waits=[cl], inc="sVE")
        csd = A("sc", lambda e: e.dma_start(out=rowd[0:1, 7168:7232],
                                            in_=clrow[:, :]),
                waits=[cs], inc="sST")
        ccl = A("gp", lambda e: e.dma_start(
            out=clcol[:, :], in_=rowd[0:1, 7168:7232].rearrange("o c -> c o"),
            allow_slow_non_contiguous=True), waits=[csd], inc="sCA")
        q1 = A("ve", lambda e: e.tensor_scalar(
            qsel01[:, :], qcnt[:, :], 0.5, None, ALU.is_ge),
            waits=[t_h], inc="sVE")
        q2 = A("te", lambda e: e.transpose(pb[6][0:16, 0:128], qsel01[:, :],
                                           iden[:, :]),
               waits=pw + [q1, t_iden], inc="sTE")
        q3 = A("ve", lambda e: e.tensor_copy(qtr[:, :], pb[6][0:16, 0:128]),
               waits=[q2], inc="sVE")
        q4 = A("sc", lambda e: e.dma_start(
            out=rowd[0:1, 4096:6144].rearrange("o (t p) -> t p", p=128),
            in_=qtr[:, :]), waits=[q3], inc="sST")
        qselb = f32u[:, 4096:6144]
        q5 = A("sy", lambda e: e.dma_start(
            out=qselb[:, :], in_=rowd[0:1, 4096:6144].partition_broadcast(128)),
            waits=[q4], inc="sLD")
        return ccl, q5, qselb, cs

    def rowsel_mms(tok_ccl, tok_oh, pw):
        oh = bfu[0:64, bC:bC + 4096]
        mm = None
        for ot in range(NT):
            mm = A("te", lambda e, ot=ot: e.matmul(
                pb[5][:, ot:ot + 1], oh[:, ot * 128:(ot + 1) * 128], clcol[:, :],
                start=True, stop=True, skip_group_check=True),
                waits=pw + [tok_ccl, tok_oh],
                inc="sTE" if ot == NT - 1 else None)
        r1 = A("ve", lambda e: e.tensor_scalar_mul(rsel[:, :], pb[5][:, 0:NT], 0.5),
               waits=[mm], inc="sVE")
        r2 = A("ve", lambda e: e.tensor_mul(brsel[:, :], bias_s[:, :], rsel[:, :]),
               waits=[r1, t_bias], inc="sVE")
        return r2

    def dense_ffn(tok_h, tok_sel, tok_qselb, qselb, resid_src, dst, wsrc, pw):
        wbuf = [bfu[:, bD_:bD_ + 4096], bfu[:, bE:bE + 4096]]
        xst = [f32u[:, 0:2048], f32u[:, 2048:4096]]
        uev = [f32u[:, 6144:6656], f32u[:, 6656:7168]]
        xwr = [f32u[:, 7168:7680], f32u[:, 7680:8192]]
        t_w = [None, None]
        t_x = [None, None]
        t_wfree = [None, None]
        t_xfree = [None, None]
        t_ufree = [None, None]
        t_wrfree = [None, None]
        t_bfree = [None, None]
        ot_toks = [None] * NT
        st = None
        for ot in range(NT):
            ws = ot % 2
            t_w[ws] = A("gp", lambda e, ot=ot, ws=ws: e.dma_start(
                out=wbuf[ws][:, :], in_=wsrc[ot].rearrange("p t o -> p (t o)")),
                waits=pw + [t_wfree[ws]], inc="sCA")
            t_x[ws] = A("sy", lambda e, ot=ot, ws=ws: e.dma_start(
                out=xst[ws][:, :], in_=resid_src[ot * 128:(ot + 1) * 128, :]),
                waits=pw + [t_xfree[ws]], inc="sLD")
            for rc in range(4):
                i = ot * 4 + rc
                bs = i % 2
                bk = pb[bs]
                mm = None
                for t in range(NT):
                    mm = A("te", lambda e, bk=bk, t=t, rc=rc, ws=ws: e.matmul(
                        bk[:, :], wbuf[ws][:, t * 128:(t + 1) * 128],
                        big[:, t * R + rc * 512: t * R + rc * 512 + 512],
                        start=(t == 0), stop=(t == NT - 1), skip_group_check=True),
                        waits=pw + [tok_h[t], t_w[ws] if t == 0 else None,
                                    t_bfree[bs] if t == 0 else None],
                        inc="sTE" if t == NT - 1 else None)
                if rc == 3:
                    t_wfree[ws] = mm
                ev = A("sc", lambda e, bk=bk, bs=bs, ot=ot: e.activation(
                    uev[bs][:, :], bk[:, :], AF.Identity,
                    bias=brsel[:, ot:ot + 1], scale=rsel[:, ot:ot + 1]),
                    waits=pw + [mm, tok_sel, t_ufree[bs]], inc="sSC")
                t_bfree[bs] = ev
                m1 = A("ve", lambda e, bs=bs, rc=rc: e.tensor_mul(
                    uev[bs][:, :], uev[bs][:, :],
                    qselb[:, rc * 512:(rc + 1) * 512]),
                    waits=pw + [ev, tok_qselb], inc="sVE")
                m2 = A("ve", lambda e, bs=bs, rc=rc, ws=ws: e.tensor_add(
                    xwr[bs][:, :], uev[bs][:, :],
                    xst[ws][:, rc * 512:(rc + 1) * 512]),
                    waits=[m1, t_x[ws], t_wrfree[bs]], inc="sVE")
                t_ufree[bs] = m2
                if rc == 3:
                    t_xfree[ws] = m2
                st = A("sc", lambda e, bs=bs, ot=ot, rc=rc: e.dma_start(
                    out=dst[ot * 128:(ot + 1) * 128, rc * 512:(rc + 1) * 512],
                    in_=xwr[bs][:, :]), waits=[m2], inc="sST")
                t_wrfree[bs] = st
            ot_toks[ot] = st
        return ot_toks, st

    def qk_proj(tok_h, wsrc, dst, t_cos, t_sin, pw):
        cosb = f32u[:, 0:2048]
        sinb = f32u[:, 2048:4096]
        wbuf = [bfu[:, bD_:bD_ + 4096], bfu[:, bE:bE + 4096]]
        qraw = [f32u[:, 4096:6144], f32u[:, 6144:8192]]
        tmp = bfu[:, bC:bC + 4096].bitcast(F32)
        qout = [bfu[:, bA:bA + 2048], bfu[:, bA + 2048:bA + 4096]]
        t_w = [None, None]
        t_wfree = [None, None]
        t_qfree = [None, None]
        t_bfree = [None, None]
        t_ofree = [None, None]
        t_tfree = None
        last = None
        for u in range(NT):
            ws = u % 2
            qs = u % 2
            t_w[ws] = A("gp", lambda e, u=u, ws=ws: e.dma_start(
                out=wbuf[ws][:, :], in_=wsrc[u].rearrange("p t o -> p (t o)")),
                waits=pw + [t_wfree[ws]], inc="sCA")
            evs = []
            for rc in range(4):
                bs = rc % 2
                bk = pb[bs]
                mm = None
                for t in range(NT):
                    mm = A("te", lambda e, bk=bk, t=t, rc=rc, ws=ws: e.matmul(
                        bk[:, :], wbuf[ws][:, t * 128:(t + 1) * 128],
                        big[:, t * R + rc * 512: t * R + rc * 512 + 512],
                        start=(t == 0), stop=(t == NT - 1), skip_group_check=True),
                        waits=pw + [tok_h[t], t_w[ws] if t == 0 else None,
                                    t_bfree[bs] if t == 0 else None],
                        inc="sTE" if t == NT - 1 else None)
                if rc == 3:
                    t_wfree[ws] = mm
                ev = A("sc", lambda e, bk=bk, qs=qs, rc=rc: e.activation(
                    qraw[qs][:, rc * 512:(rc + 1) * 512], bk[:, :], AF.Identity,
                    bias=0.0, scale=1.0),
                    waits=pw + [mm, t_qfree[qs] if rc == 0 else None], inc="sSC")
                t_bfree[bs] = ev
                evs.append(ev)
            s1 = A("sy", lambda e, qs=qs: e.dma_start(
                out=tmp[0:64, :], in_=qraw[qs][64:128, :]),
                waits=pw + evs + [t_tfree], inc="sLD")
            s2 = A("sy", lambda e, qs=qs: e.dma_start(
                out=tmp[64:128, :], in_=qraw[qs][0:64, :]), inc="sLD")
            r1 = A("ve", lambda e: e.tensor_mul(tmp[:, :], tmp[:, :], sinb[:, :]),
                   waits=pw + [s1, s2, t_sin], inc="sVE")
            r2 = A("ve", lambda e, qs=qs: e.tensor_mul(
                qraw[qs][:, :], qraw[qs][:, :], cosb[:, :]),
                waits=evs + [t_cos, s1, s2], inc="sVE")
            r3 = A("ve", lambda e, qs=qs: e.tensor_add(
                qout[qs][:, :], qraw[qs][:, :], tmp[:, :]),
                waits=[r1, r2, t_ofree[qs]], inc="sVE")
            t_qfree[qs] = r3
            t_tfree = r3
            last = A("sc", lambda e, u=u, qs=qs: e.dma_start(
                out=dst[u * 128:(u + 1) * 128, :], in_=qout[qs][:, :]),
                waits=[r3], inc="sST")
            t_ofree[qs] = last
        return last

    def v_proj(tok_h, pw):
        wvb = bfu[:, 0:16384]
        vev = [bfu[:, bF:bF + 512], bfu[:, bF + 512:bF + 1024]]
        t_bfree = [None, None]
        t_vfree = [None, None]
        last = None
        g_toks = [None] * 4
        t_wprev = None
        for hc in range(8):
            t_w = A("gp", lambda e, hc=hc: e.dma_start(
                out=wvb[:, :], in_=wv_t[hc].rearrange("p t o -> p (t o)")),
                waits=pw + [t_wprev], inc="sCA")
            for rt in range(NRT):
                bs = rt % 2
                bk = pb[bs]
                mm = None
                for t in range(NT):
                    mm = A("te", lambda e, bk=bk, t=t, rt=rt: e.matmul(
                        bk[:, :], big[:, t * R + rt * 128: t * R + rt * 128 + 128],
                        wvb[:, t * 512:(t + 1) * 512],
                        start=(t == 0), stop=(t == NT - 1), skip_group_check=True),
                        waits=pw + [tok_h[t], t_w if t == 0 else None,
                                    t_bfree[bs] if t == 0 else None],
                        inc="sTE" if t == NT - 1 else None)
                if rt == NRT - 1:
                    t_wprev = mm
                ev = A("sc", lambda e, bk=bk, vs=rt % 2: e.activation(
                    vev[vs][:, :], bk[:, :], AF.Identity, bias=0.0, scale=1.0),
                    waits=pw + [mm, t_vfree[rt % 2]], inc="sSC")
                t_bfree[bs] = ev
                stv = A("sc", lambda e, rt=rt, hc=hc, vs=rt % 2: e.dma_start(
                    out=vd[hc // 2][rt * 128:(rt + 1) * 128,
                                    (hc % 2) * 512:(hc % 2) * 512 + 512],
                    in_=vev[vs][:, :]), waits=[ev], inc="sST")
                t_vfree[rt % 2] = stv
                last = stv
            g_toks[hc // 2] = last
        return g_toks

    def attention(t_agk, t_agv, pw):
        kth = [bfu[:, bA:bA + 4096], bfu[:, bB:bB + 4096]]
        vh = [bfu[:, bC:bC + 4096], bfu[:, bD_:bD_ + 4096]]
        qth = [bfu[:, bE:bE + 2048], bfu[:, bE + 2048:bE + 4096]]
        expall = f32u[:, 2048:2816].bitcast(BF16)
        expb = [expall[:, 512 * i:512 * (i + 1)] for i in range(3)]
        rrow = f32u[0:1, 0:512]
        recipb = [f32u[:, 512:1024], f32u[:, 1024:1536]]
        dacc = f32u[:, 1536:2048]
        ctx_banks = [0, 1, 4]
        lbanks = [2, 3, 6]
        dn = pb[5]
        t_kvfree = [None, None]
        t_exp_pv = [None] * 3
        t_exp_da = [None] * 3
        t_ctxfree = [None] * 3
        t_dnfree = None
        t_rbfree = [None, None]
        t_rrowfree = None
        t_b7free = None
        t_exq = [None] * 3
        tok_ctx = {}
        kv_toks = {}
        gi = 0

        def load_head(h):
            hs = h % 2
            g = h // 8
            w = pw + [t_kvfree[hs], t_agk, t_agv[g]]
            toks = []
            for r in range(2):
                toks.append(A("sy", lambda e, h=h, r=r, hs=hs: e.dma_start(
                    out=kth[hs].rearrange("p (i g c) -> p i g c", g=2, c=128)[:, :, r, :],
                    in_=kg[r * D + h * 128: r * D + (h + 1) * 128, :]
                        .rearrange("p (i c) -> p i c", c=128)),
                    waits=w, inc="sLD"))
                for sp in range(2):
                    toks.append(A("sy", lambda e, h=h, r=r, sp=sp, hs=hs, g=g: e.dma_start(
                        out=vh[hs].rearrange("p (s i g c) -> p s i g c",
                                             s=2, g=2, c=128)[:, sp, :, r, :],
                        in_=vg[g][r * R + sp * 1024: r * R + (sp + 1) * 1024,
                                  (h % 8) * 128:(h % 8) * 128 + 128]
                            .rearrange("(i p) c -> p i c", p=128)),
                        waits=w, inc="sLD"))
            toks.append(A("sy", lambda e, h=h, hs=hs: e.dma_start(
                out=qth[hs][:, :], in_=qTd[h * 128:(h + 1) * 128, :]),
                waits=w, inc="sLD"))
            kv_toks[h] = toks

        load_head(0)
        load_head(1)
        for h in range(H):
            hs = h % 2
            toks_h = kv_toks[h]
            last_pe = None
            for qc in range(4):
                nkb = 8 * qc + 8
                cb = ctx_banks[gi % 3]
                t_da = None
                srcs = [None] * nkb
                geom = []
                for kb in range(nkb):
                    j = kb - 8 * qc
                    if j < 0:
                        geom.append((0, 512))
                    else:
                        off = (j // 2) * 128
                        geom.append((off, 512 - off))

                def emit_front(kb):
                    nonlocal t_da
                    off, N = geom[kb]
                    j = kb - 8 * qc
                    es = kb % 3
                    lbk = pb[lbanks[es]]
                    mm = A("te", lambda e, lbk=lbk, kb=kb, qc=qc, off=off, N=N,
                           hs=hs: e.matmul(
                        lbk[:, off:off + N], kth[hs][:, kb * 128:(kb + 1) * 128],
                        qth[hs][:, qc * 512 + off: qc * 512 + off + N],
                        start=True, stop=True, skip_group_check=True),
                        waits=pw + toks_h + [t_exq[es]], inc="sTE")
                    ex = A("sc", lambda e, lbk=lbk, es=es, off=off, N=N: e.activation(
                        expb[es][:, off:off + N], lbk[:, off:off + N], AF.Exp,
                        bias=0.0, scale=ISQ),
                        waits=pw + [mm, t_exp_pv[es], t_exp_da[es]], inc="sSC")
                    t_exq[es] = ex
                    src = ex
                    if j >= 0:
                        m = j // 2
                        mi = j - 2 * m
                        src = A("gp", lambda e, es=es, m=m, mi=mi: e.tensor_mul(
                            expb[es][:, m * 128:(m + 1) * 128],
                            expb[es][:, m * 128:(m + 1) * 128],
                            mask2[:, mi * 128:(mi + 1) * 128]),
                            waits=pw + [ex, t_mask], inc="sGP")
                    if kb == 0:
                        t_da = A("ve", lambda e, es=es: e.tensor_copy(
                            dacc[:, :], expb[es][:, :]),
                            waits=pw + [src, t_dnfree], inc="sVE")
                    else:
                        t_da = A("ve", lambda e, es=es, off=off, N=N: e.tensor_add(
                            dacc[:, off:off + N], dacc[:, off:off + N],
                            expb[es][:, off:off + N]),
                            waits=[src, t_da], inc="sVE")
                    t_exp_da[es] = t_da
                    srcs[kb] = src

                for kb in range(min(3, nkb)):
                    emit_front(kb)
                for kb in range(nkb):
                    off, N = geom[kb]
                    es = kb % 3
                    pv = A("te", lambda e, es=es, off=off, N=N, kb=kb, nkb=nkb,
                           cb=cb, hs=hs: e.matmul(
                        pb[cb][:, off:off + N], vh[hs][:, kb * 128:(kb + 1) * 128],
                        expb[es][:, off:off + N],
                        start=(kb == 0), stop=(kb == nkb - 1),
                        skip_group_check=True),
                        waits=[srcs[kb], t_ctxfree[gi % 3] if kb == 0 else None],
                        inc="sTE")
                    t_exp_pv[es] = pv
                    last_pe = pv
                    if kb + 3 < nkb:
                        emit_front(kb + 3)
                dnm = A("te", lambda e: e.matmul(
                    dn[0:1, 0:512], ones_f[:, :], dacc[:, :],
                    start=True, stop=True, skip_group_check=True),
                    waits=pw + [t_da, t_ones, t_dnfree], inc="sTE")
                t_dnfree_mm = dnm
                rc1a = A("sc", lambda e: e.activation(rrow[:, :], dn[0:1, 0:512],
                                                      AF.Ln),
                         waits=pw + [dnm, t_rrowfree], inc="sSC")
                rc1 = A("sc", lambda e: e.activation(rrow[:, :], rrow[:, :],
                                                     AF.Exp, bias=0.0, scale=-1.0),
                        waits=[rc1a], inc="sSC")
                t_dnfree = rc1a
                rs = gi % 2
                rbm = A("te", lambda e: e.matmul(
                    pb[7][:, 0:512], ones_r[:, :], rrow[:, :],
                    start=True, stop=True, skip_group_check=True),
                    waits=pw + [rc1, t_onesr, t_b7free], inc="sTE")
                t_rrowfree = rbm
                rd2 = A("ve", lambda e, rs=rs: e.tensor_copy(
                    recipb[rs][:, :], pb[7][:, 0:512]),
                    waits=pw + [rbm, t_rbfree[rs]], inc="sVE")
                t_b7free = rd2
                nmt = A("ve", lambda e, cb=cb, h=h, qc=qc, rs=rs: e.tensor_mul(
                    big[:, h * R + qc * 512: h * R + (qc + 1) * 512],
                    pb[cb][:, :], recipb[rs][:, :]),
                    waits=pw + [rd2, last_pe], inc="sVE")
                t_rbfree[rs] = nmt
                t_ctxfree[gi % 3] = nmt
                tok_ctx[(h, qc)] = nmt
                gi += 1
            t_kvfree[hs] = last_pe
            if h + 2 < H:
                load_head(h + 2)
        return tok_ctx

    def wo_proj(tok_ctx, pw):
        wbuf = [bfu[:, bD_:bD_ + 4096], bfu[:, bE:bE + 4096]]
        x1rd = [f32u[:, 0:2048], f32u[:, 2048:4096]]
        xwr = [f32u[:, 6144:6656], f32u[:, 6656:7168]]
        t_w = [None, None]
        t_x = [None, None]
        t_wfree = [None, None]
        t_xfree = [None, None]
        t_wrfree = [None, None]
        t_bfree = [None, None]
        ot_toks = [None] * NT
        for ot in range(NT):
            ws = ot % 2
            t_w[ws] = A("gp", lambda e, ot=ot, ws=ws: e.dma_start(
                out=wbuf[ws][:, :], in_=wo_t[ot].rearrange("p t o -> p (t o)")),
                waits=pw + [t_wfree[ws]], inc="sCA")
            t_x[ws] = A("sy", lambda e, ot=ot, ws=ws: e.dma_start(
                out=x1rd[ws][:, :], in_=x1T[ot * 128:(ot + 1) * 128, :]),
                waits=pw + [t_xfree[ws]], inc="sLD")
            st = None
            for rc in range(4):
                i = ot * 4 + rc
                bs = i % 2
                bk = pb[bs]
                mm = None
                for t in range(NT):
                    mm = A("te", lambda e, bk=bk, t=t, rc=rc, ws=ws: e.matmul(
                        bk[:, :], wbuf[ws][:, t * 128:(t + 1) * 128],
                        big[:, t * R + rc * 512: t * R + rc * 512 + 512],
                        start=(t == 0), stop=(t == NT - 1), skip_group_check=True),
                        waits=pw + [tok_ctx[(t, rc)], t_w[ws] if t == 0 else None,
                                    t_bfree[bs] if t == 0 else None],
                        inc="sTE" if t == NT - 1 else None)
                if rc == 3:
                    t_wfree[ws] = mm
                m2 = A("ve", lambda e, bk=bk, bs=bs, rc=rc, ws=ws: e.tensor_add(
                    xwr[bs][:, :], bk[:, :], x1rd[ws][:, rc * 512:(rc + 1) * 512]),
                    waits=pw + [mm, t_x[ws], t_wrfree[bs]], inc="sVE")
                t_bfree[bs] = m2
                if rc == 3:
                    t_xfree[ws] = m2
                st = A("sc", lambda e, bs=bs, ot=ot, rc=rc: e.dma_start(
                    out=x2T[ot * 128:(ot + 1) * 128, rc * 512:(rc + 1) * 512],
                    in_=xwr[bs][:, :]), waits=[m2], inc="sST")
                t_wrfree[bs] = st
            ot_toks[ot] = st
        return ot_toks

    t_oh1 = build_onehot([])

    tok_h1 = ln_pass(x0T, None, [t_oh1, t_afd])
    pwa = bld.jall()
    tok_ccl1, tok_qselb1, qselb1, tok_cs1 = dots_sels(tok_h1, 0, pwa)
    tok_sel1 = rowsel_mms(tok_ccl1, t_oh1, pwa)
    pwd = bld.jall()
    if _STOP == "dots":
        dfu = nc.dram_tensor("dbg_f32u", [128, 8192], F32, kind="ExternalOutput").ap()
        t_f1 = A("sc", lambda e: e.dma_start(out=dfu, in_=f32u[:, :]),
                 waits=pwd, inc="sST")
        st1_toks = [t_f1] * NT
    else:
        st1_toks, t_f1 = dense_ffn(tok_h1, tok_sel1, tok_qselb1, qselb1,
                                   x0T, x1T, wf_t, pwd)

    if _STOP in ("ffn1", "dots"):
        t_f2 = t_f1
    pw = bld.jall()
    if _STOP in ("ffn1", "dots"):
        tok_h2 = None
    elif True:
        tok_h2 = ln_pass(x1T, st1_toks, pw)
    if _STOP not in ("ffn1", "dots"):
        pw2 = bld.jall()
        t_cos = A("sy", lambda e: e.dma_start(out=f32u[:, 0:2048], in_=cos_t),
                  waits=pw2, inc="sLD")
        t_sin = A("sy", lambda e: e.dma_start(out=f32u[:, 2048:4096], in_=sin_t),
                  waits=pw2, inc="sLD")
        t_q = qk_proj(tok_h2, wq_t, qTd, t_cos, t_sin, pw2)
        t_k = qk_proj(tok_h2, wk_t, kTd, t_cos, t_sin, [t_q])
        pw3 = bld.jall()
        t_agk = A("gp", lambda e: e.collective_compute(
            "AllGather", ALU.bypass, replica_groups=[[0, 1], [2, 3], [4, 5], [6, 7]],
            ins=[kTd.opt()], outs=[kg.opt()]), waits=[t_k], inc="sCC", amt=1)
        tv_g = v_proj(tok_h2, pw3)
        t_v = tv_g[3]
        if _STOP == "qkv":
            t_f2 = t_v
    if _STOP not in ("ffn1", "dots", "qkv"):
        t_agv = [None] * 4
        for g in range(4):
            t_agv[g] = A("gp", lambda e, g=g: e.collective_compute(
                "AllGather", ALU.bypass,
                replica_groups=[[0, 1], [2, 3], [4, 5], [6, 7]],
                ins=[vd[g].opt()], outs=[vg[g].opt()]),
                waits=[tv_g[g]], inc="sCC", amt=1)

        pw4 = [t for t in bld.jall() if t[0] != "sCC"] + [t_agk]
        tok_ctx = attention(t_agk, t_agv, pw4)
        if _STOP == "attn":
            pwz2 = bld.jall()
            dctx = nc.dram_tensor("dbg_ctx", [128, NT * R], BF16,
                                  kind="ExternalOutput").ap()
            t_f2 = A("sc", lambda e: e.dma_start(out=dctx, in_=big[:, :]),
                     waits=pwz2, inc="sST")
    if _STOP not in ("ffn1", "dots", "qkv", "attn"):
        pw5 = bld.jall()
        st2_toks = wo_proj(tok_ctx, pw5)
        if _STOP == "wo":
            t_f2 = st2_toks[NT - 1]
    if _STOP not in ("ffn1", "dots", "qkv", "attn", "wo"):
        pw6 = bld.jall()
        t_oh2 = build_onehot(pw6)
        tok_h3 = ln_pass(x2T, st2_toks, pw6 + [t_oh2])
        pw7 = bld.jall()
        tok_ccl2, tok_qselb2, qselb2, tok_cs2 = dots_sels(tok_h3, 1, pw7)
        tok_sel2 = rowsel_mms(tok_ccl2, t_oh2, pw7)
        pw8 = bld.jall()
        _, t_f2 = dense_ffn(tok_h3, tok_sel2, tok_qselb2, qselb2,
                            x2T, outT, wf_t, pw8)

    t_dbg = None
    if _DBG:
        scr = {"x1T": (x1T, [D, R], F32), "x2T": (x2T, [D, R], F32),
               "qTd": (qTd, [D, R], BF16), "kTd": (kTd, [D, R], BF16),
               "rowd": (rowd, [1, 8192], F32)}
        pwz = bld.jall()
        for nm in _DBG:
            if nm == "big":
                dbig = nc.dram_tensor("dbg_big", [128, NT * R], BF16,
                                      kind="ExternalOutput").ap()
                t_dbg = A("sc", lambda e, dbig=dbig: e.dma_start(
                    out=dbig, in_=big[:, :]), waits=pwz + [t_f2], inc="sST")
            else:
                ap, shp, dt_ = scr[nm]
                dd = nc.dram_tensor("dbg_" + nm, shp, dt_,
                                    kind="ExternalOutput").ap()
                t_dbg = A("sc", lambda e, ap=ap, dd=dd: e.dma_start(
                    out=dd, in_=ap), waits=pwz + [t_f2], inc="sST")

    A("sc", lambda e: None, waits=[t_f2, t_dbg])

    with nc.Block() as block:
        def run(engname, engobj):
            hi = {}
            for waits, fn, inc, amt in bld.q[engname]:
                for (s, v) in waits:
                    if hi.get(s, -1) >= v:
                        continue
                    engobj.wait_ge(handles[s], v)
                    hi[s] = v
                inst = fn(engobj)
                if inc is not None:
                    inst.then_inc(handles[inc], amt)

        @block.tensor
        def _(t):
            run("te", t)

        @block.vector
        def _(v):
            run("ve", v)

        @block.scalar
        def _(s_):
            run("sc", s_)

        @block.gpsimd
        def _(g):
            run("gp", g)

        @block.sync
        def _(s_):
            run("sy", s_)

    ctx.close()
    return nc



def _prep_shared(wq, wk, wv, wo, W, centroids, bias, assignments):
    c = np.ascontiguousarray
    wq_t = c(wq.reshape(NT, 128, NT, 128).transpose(2, 1, 0, 3))
    wk_t = c(wk.reshape(NT, 128, NT, 128).transpose(2, 1, 0, 3))
    wo_t = c(wo.reshape(NT, 128, NT, 128).transpose(2, 1, 0, 3))
    wf_t = c(W.T.reshape(NT, 128, NT, 128).transpose(2, 1, 0, 3))
    wv_t = c(wv.reshape(NT, 128, 8, 512).transpose(2, 1, 0, 3))
    ct_t = c(centroids.T.reshape(NT, 128, NCL).transpose(1, 0, 2))
    bias_t = c(bias.reshape(NT, 128).T)
    asn_t = np.ascontiguousarray(assignments[None, :].astype(np.int32))
    iden_t = np.eye(128, dtype=np.float32)
    return dict(wq_t=wq_t, wk_t=wk_t, wv_t=wv_t, wo_t=wo_t, wf_t=wf_t,
                ct_t=ct_t, bias_t=bias_t, asn_t=asn_t, iden_t=iden_t)


def _prep_half(half):
    inv = (1.0 / (10000.0 ** (np.arange(0, 128, 2, dtype=np.float64) / 128.0)))
    bi = np.arange(R) // 128
    pp = np.arange(R) % 128
    pos = (2 * bi + half) * 128 + pp
    ang = pos[None, :] * inv[(np.arange(128) % 64)][:, None]
    cos_t = np.cos(ang).astype(np.float32)
    sin = np.sin(ang)
    sgn = np.where(np.arange(128) < 64, -1.0, 1.0)
    sin_t = (sin * sgn[:, None]).astype(np.float32)
    tri = (np.arange(128)[:, None] <= np.arange(128)[None, :]).astype(np.float32)
    if half == 0:
        m0, m1 = tri, np.zeros((128, 128), np.float32)
    else:
        m0, m1 = np.ones((128, 128), np.float32), tri
    mask_d = np.concatenate([m0, m1], axis=1).astype(ml_dtypes.bfloat16)
    return dict(cos_t=cos_t, sin_t=sin_t, mask_d=mask_d)


_ROWIDX = {}


def _rows(half):
    if half not in _ROWIDX:
        idx = []
        for bi in range(16):
            j = 2 * bi + half
            idx.extend(range(j * 128, (j + 1) * 128))
        _ROWIDX[half] = np.asarray(idx)
    return _ROWIDX[half]


def kernel(L0_attn_hidden_states, L0_attn_wq, L0_attn_wk, L0_attn_wv,
           L0_attn_wo, L0_attn_kq1, L0_attn_kq2, L0_attn_kk1, L0_attn_kk2,
           L0_attn_scalingD, L0_attn_interaction_k, L0_attn_scalingD2,
           L0_ffn_weight, L0_ffn_bias, L0_ffn_centroids, L0_ffn_assignments):
    x = np.asarray(L0_attn_hidden_states, dtype=np.float32)
    shared = _prep_shared(np.asarray(L0_attn_wq, np.float32),
                          np.asarray(L0_attn_wk, np.float32),
                          np.asarray(L0_attn_wv, np.float32),
                          np.asarray(L0_attn_wo, np.float32),
                          np.asarray(L0_ffn_weight, np.float32),
                          np.asarray(L0_ffn_centroids, np.float32),
                          np.asarray(L0_ffn_bias, np.float32),
                          np.asarray(L0_ffn_assignments))
    halves = [_prep_half(0), _prep_half(1)]

    nc = _build()
    in_maps = []
    for c in range(NCORES):
        b, half = c // 2, c % 2
        x0T = np.ascontiguousarray(x[b][_rows(half), :].T)
        m = dict(shared)
        m.update(halves[half])
        m["x0T"] = x0T
        in_maps.append(m)

    res = run_bass_kernel_spmd(nc, in_maps, core_ids=list(range(NCORES)))

    out = np.empty((B, S, D), dtype=np.float32)
    for c in range(NCORES):
        b, half = c // 2, c % 2
        out[b][_rows(half), :] = np.asarray(res.results[c]["outT"]).T
    return out


# version 08
# speedup vs baseline: 1.0509x; 1.0509x over previous
import numpy as np
import ml_dtypes
import concourse.bass as bass
import concourse.mybir as mybir
from concourse.bass_utils import run_bass_kernel_spmd

F32 = mybir.dt.float32
BF16 = mybir.dt.bfloat16
I32 = mybir.dt.int32
AF = mybir.ActivationFunctionType
ALU = mybir.AluOpType
AX = mybir.AxisListType

D = 4096
B = 4
S = 4096
H = 32
NT = 32
R = 2048
NRT = 16
NCL = 64
EPS = 1e-5
ISQ = float(1.0 / np.sqrt(128.0))
NCORES = 8

bA, bB, bC, bD_, bE, bF = 0, 4096, 8192, 12288, 16384, 20480


class _B:
    ENGS = ("te", "ve", "sc", "gp", "sy")

    def __init__(self):
        self.q = {e: [] for e in self.ENGS}
        self.cnt = {}
        self.owner = {}
        for s, o in (("sLD", None), ("sCA", None), ("sST", None), ("sCC", None),
                     ("sTE", "te"), ("sVE", "ve"), ("sSC", "sc"), ("sGP", "gp")):
            self.cnt[s] = 0
            self.owner[s] = o

    def add(self, eng, fn, waits=(), inc=None, amt=None):
        w = tuple(t for t in waits if t is not None)
        if amt is None:
            amt = 16 if inc in ("sLD", "sCA", "sST") else 1
        self.q[eng].append((w, fn, inc, amt))
        if inc is not None:
            self.cnt[inc] += amt
            return (inc, self.cnt[inc])
        return None

    def jall(self):
        return [(s, c) for s, c in self.cnt.items() if c > 0]


_DBG = []
_STOP = ""
_DOTS_RT = NRT


def _build():
    nc = bass.Bass("TRN2", target_bir_lowering=False, debug=False)

    x0T = nc.dram_tensor("x0T", [D, R], F32, kind="ExternalInput").ap()
    wq_t = nc.dram_tensor("wq_t", [NT, 128, NT, 128], F32, kind="ExternalInput").ap()
    wk_t = nc.dram_tensor("wk_t", [NT, 128, NT, 128], F32, kind="ExternalInput").ap()
    wv_t = nc.dram_tensor("wv_t", [8, 128, NT, 512], F32, kind="ExternalInput").ap()
    wo_t = nc.dram_tensor("wo_t", [NT, 128, NT, 128], F32, kind="ExternalInput").ap()
    wf_t = nc.dram_tensor("wf_t", [NT, 128, NT, 128], F32, kind="ExternalInput").ap()
    ct_t = nc.dram_tensor("ct_t", [128, NT, NCL], F32, kind="ExternalInput").ap()
    bias_t = nc.dram_tensor("bias_t", [128, NT], F32, kind="ExternalInput").ap()
    cos_t = nc.dram_tensor("cos_t", [128, R], F32, kind="ExternalInput").ap()
    sin_t = nc.dram_tensor("sin_t", [128, R], F32, kind="ExternalInput").ap()
    mask_d = nc.dram_tensor("mask_d", [128, 256], BF16, kind="ExternalInput").ap()
    asn_t = nc.dram_tensor("asn_t", [1, D], I32, kind="ExternalInput").ap()
    iden_t = nc.dram_tensor("iden_t", [128, 128], F32, kind="ExternalInput").ap()
    outT = nc.dram_tensor("outT", [D, R], F32, kind="ExternalOutput").ap()

    x1T = nc.dram_tensor("x1T", [D, R], F32).ap()
    x2T = nc.dram_tensor("x2T", [D, R], F32).ap()
    qTd = nc.dram_tensor("qTd", [D, R], BF16).ap()
    kTd = nc.dram_tensor("kTd", [D, R], BF16).ap()
    vd = nc.dram_tensor("vd", [R, D], BF16).ap()
    kg = nc.dram_tensor("kg", [2 * D, R], BF16).ap()
    vg = nc.dram_tensor("vg", [2 * R, D], BF16).ap()
    rowd = nc.dram_tensor("rowd", [1, 8192], F32).ap()
    asnf = nc.dram_tensor("asnf", [1, D], F32).ap()
    cc_in = [nc.dram_tensor(f"cc_in{i}", [1, NCL], F32).ap() for i in range(2)]
    cc_out = [nc.dram_tensor(f"cc_out{i}", [1, NCL], F32).ap() for i in range(2)]

    bld = _B()
    A = bld.add

    from contextlib import ExitStack
    ctx = ExitStack()
    en = ctx.enter_context

    big = en(nc.sbuf_tensor("big", [128, NT * R], BF16))
    bfu = en(nc.sbuf_tensor("bfu", [128, 21504], BF16))
    f32u = en(nc.sbuf_tensor("f32u", [128, 8192], F32))
    mask2 = en(nc.sbuf_tensor("mask2", [128, 256], BF16))
    iden = en(nc.sbuf_tensor("iden", [128, 128], F32))
    ones_bf = en(nc.sbuf_tensor("ones_bf", [128, 1], BF16))
    ones_f = en(nc.sbuf_tensor("ones_f", [128, 1], F32))
    ones_r = en(nc.sbuf_tensor("ones_r", [1, 128], F32))
    io64 = en(nc.sbuf_tensor("io64", [64, 1], F32))
    bias_s = en(nc.sbuf_tensor("bias_s", [128, NT], F32))
    rsel = en(nc.sbuf_tensor("rsel", [128, NT], F32))
    brsel = en(nc.sbuf_tensor("brsel", [128, NT], F32))
    asn_i = f32u[0:1, 4096:8192].bitcast(I32)
    asn_f = f32u[0:1, 0:4096]
    clrow = en(nc.sbuf_tensor("clrow", [1, NCL], F32))
    clcol = en(nc.sbuf_tensor("clcol", [64, 1], BF16))
    pb = [en(nc.psum_tensor(f"pb{i}", [128, 512], F32)) for i in range(8)]

    handles = {}
    for s in bld.cnt:
        handles[s] = en(nc.semaphore(s))

    t_mask = A("sy", lambda e: e.dma_start(out=mask2[:, :], in_=mask_d), inc="sLD")
    t_iden = A("sy", lambda e: e.dma_start(out=iden[:, :], in_=iden_t), inc="sLD")
    t_bias = A("sy", lambda e: e.dma_start(out=bias_s[:, :], in_=bias_t), inc="sLD")
    t_asn = A("sy", lambda e: e.dma_start(out=asn_i[:, :], in_=asn_t), inc="sLD")
    t_ct = A("gp", lambda e: e.dma_start(
        out=bfu[:, bB:bB + 2048], in_=ct_t.rearrange("p t c -> p (t c)")), inc="sCA")
    A("ve", lambda e: e.memset(ones_bf[:, :], 1.0), inc="sVE")
    t_ones = A("ve", lambda e: e.memset(ones_f[:, :], 1.0), inc="sVE")
    t_onesr = A("ve", lambda e: e.memset(ones_r[:, :], 1.0), inc="sVE")
    t_io = A("gp", lambda e: e.iota(io64[:, :], pattern=[[0, 1]], base=0,
                                    channel_multiplier=1,
                                    allow_small_or_imprecise_dtypes=True), inc="sGP")
    t_af = A("ve", lambda e: e.tensor_copy(asn_f[:, :], asn_i[:, :]),
             waits=[t_asn], inc="sVE")
    t_afd = A("sc", lambda e: e.dma_start(out=asnf, in_=asn_f[:, :]),
              waits=[t_af], inc="sST")


    def build_onehot(pw):
        asnb = f32u[0:64, 1024:5120]
        oh = bfu[0:64, bC:bC + 4096]
        a1 = A("sy", lambda e: e.dma_start(out=asnb[:, :],
                                           in_=asnf.partition_broadcast(64)),
               waits=pw + [t_afd], inc="sLD")
        a2 = A("ve", lambda e: e.tensor_scalar(oh[:, :], asnb[:, :], io64[:, :],
                                               None, ALU.is_equal),
               waits=pw + [a1, t_io], inc="sVE")
        return a2

    def ln_pass(src, src_toks, pw):
        xld = [f32u[:, 0:2048], f32u[:, 2048:4096]]
        xb = bfu[:, bA:bA + 2048]
        xsq = bfu[:, bA + 2048:bA + 4096]
        t_cast = [None, None]
        t_mm_last = None
        t_sq = None
        for t in range(NT):
            s = t % 2
            tl = A("sy", lambda e, t=t, s=s: e.dma_start(
                out=xld[s][:, :], in_=src[t * 128:(t + 1) * 128, :]),
                waits=pw + [t_cast[s], src_toks[t] if src_toks else None], inc="sLD")
            tc = A("ve", lambda e, s=s: e.tensor_copy(xb[:, :], xld[s][:, :]),
                   waits=pw + [tl, t_mm_last], inc="sVE")
            t_cast[s] = tc
            t_sq = A("ve", lambda e: e.tensor_mul(xsq[:, :], xb[:, :], xb[:, :]),
                     waits=[tc], inc="sVE")
            for rc in range(4):
                bk = pb[4 + rc]
                A("te", lambda e, bk=bk, rc=rc, t=t: e.matmul(
                    bk[0:1, :], ones_bf[:, :], xb[:, rc * 512:(rc + 1) * 512],
                    start=(t == 0), stop=False, skip_group_check=True),
                    waits=pw + [tc])
                tm = A("te", lambda e, bk=bk, rc=rc, t=t: e.matmul(
                    bk[32:33, :], ones_bf[:, :], xsq[:, rc * 512:(rc + 1) * 512],
                    start=(t == 0), stop=(t == NT - 1), skip_group_check=True),
                    waits=[t_sq], inc="sTE" if (rc == 3) else None)
                if rc == 3:
                    t_mm_last = tm
        mrow = f32u[0:1, 4096:6144]
        rrow = f32u[0:1, 6144:8192]
        tmpr = f32u[0:1, 0:2048]
        tok_ev = []
        for rc in range(4):
            bk = pb[4 + rc]
            sl = slice(rc * 512, (rc + 1) * 512)
            A("ve", lambda e, bk=bk, sl=sl: e.tensor_scalar_mul(
                mrow[:, sl], bk[0:1, 0:512], 1.0 / D), waits=[t_mm_last], inc="sVE")
            e2 = A("ve", lambda e, bk=bk, sl=sl: e.tensor_scalar_mul(
                rrow[:, sl], bk[32:33, 0:512], 1.0 / D), waits=[t_mm_last], inc="sVE")
            tok_ev.append(e2)
        t1 = A("ve", lambda e: e.tensor_mul(tmpr[:, :], mrow[:, :], mrow[:, :]),
               waits=tok_ev, inc="sVE")
        t2 = A("ve", lambda e: e.tensor_sub(rrow[:, :], rrow[:, :], tmpr[:, :]),
               waits=[t1], inc="sVE")
        t2b = A("ve", lambda e: e.tensor_scalar_add(rrow[:, :], rrow[:, :], EPS),
                waits=[t2], inc="sVE")
        t3 = A("sc", lambda e: e.activation(tmpr[:, :], rrow[:, :], AF.Ln),
               waits=[t2b], inc="sSC")
        t4 = A("sc", lambda e: e.activation(rrow[:, :], tmpr[:, :], AF.Exp,
                                            bias=0.0, scale=-0.5),
               waits=[t3], inc="sSC")
        meanb = f32u[:, 4096:6144]
        rstdb = f32u[:, 6144:8192]
        b1 = b2 = None
        t_bf = [tok_ev[-1], t4]
        for rc in range(4):
            sl = slice(rc * 512, (rc + 1) * 512)
            mb = A("te", lambda e, sl=sl: e.matmul(
                pb[4][:, 0:512], ones_r[:, :], mrow[:, sl],
                start=True, stop=True, skip_group_check=True),
                waits=[tok_ev[-1], t1, t_onesr, t_bf[0]], inc="sTE")
            b1 = A("ve", lambda e, sl=sl: e.tensor_copy(meanb[:, sl],
                                                        pb[4][:, 0:512]),
                   waits=[mb], inc="sVE")
            t_bf[0] = b1
            rb = A("te", lambda e, sl=sl: e.matmul(
                pb[5][:, 0:512], ones_r[:, :], rrow[:, sl],
                start=True, stop=True, skip_group_check=True),
                waits=[t4, t_onesr, t_bf[1]], inc="sTE")
            b2 = A("ve", lambda e, sl=sl: e.tensor_copy(rstdb[:, sl],
                                                        pb[5][:, 0:512]),
                   waits=[rb], inc="sVE")
            t_bf[1] = b2
        xrd = [f32u[:, 0:2048], f32u[:, 2048:4096]]
        t_n2 = [t_cast[0], t_cast[1]]
        tok_h = [None] * NT
        for t in range(NT):
            s = t % 2
            tl = A("sy", lambda e, t=t, s=s: e.dma_start(
                out=xrd[s][:, :], in_=src[t * 128:(t + 1) * 128, :]),
                waits=[t_n2[s], t4 if t < 2 else None], inc="sLD")
            n1 = A("ve", lambda e, s=s: e.tensor_sub(xrd[s][:, :], xrd[s][:, :],
                                                     meanb[:, :]),
                   waits=[tl, b1], inc="sVE")
            n2 = A("ve", lambda e, s=s, t=t: e.tensor_mul(
                big[:, t * R:(t + 1) * R], xrd[s][:, :], rstdb[:, :]),
                waits=[n1, b2], inc="sVE")
            t_n2[s] = n2
            tok_h[t] = n2
        return tok_h

    def dots_sels(tok_h, cc_idx, pw):
        ctb = bfu[:, bB:bB + 2048]
        hits = f32u[:, 0:64]
        expd = f32u[:, 64:128]
        negm = f32u[:, 128:129]
        scol = f32u[:, 130:131]
        thr = f32u[:, 132:133]
        qcnt = f32u[:, 192:208]
        qsel01 = f32u[:, 208:224]
        qtr = f32u[0:16, 224:352]
        cntrow = f32u[0:1, 6144:6208]
        t_ex = [None, None]
        t_ex_last = None
        t_h = None
        t_cnt = None
        for rt in range(_DOTS_RT):
            bk = pb[2 + rt % 2]
            mm = None
            for t in range(NT):
                mm = A("te", lambda e, bk=bk, t=t, rt=rt: e.matmul(
                    bk[:, 0:NCL], big[:, t * R + rt * 128: t * R + rt * 128 + 128],
                    ctb[:, t * NCL:(t + 1) * NCL],
                    start=(t == 0), stop=(t == NT - 1), skip_group_check=True),
                    waits=pw + [tok_h[t], t_ct if (t == 0 and rt == 0) else None,
                                t_ex[rt % 2] if t == 0 else None],
                    inc="sTE" if t == NT - 1 else None)
            nm = A("ve", lambda e, bk=bk: e.reduce_max(
                negm[:, :], bk[:, 0:NCL], axis=AX.X, negate=True),
                waits=[mm, t_ex_last], inc="sVE")
            nm10 = A("ve", lambda e: e.tensor_scalar_mul(negm[:, :], negm[:, :],
                                                         10.0),
                     waits=[nm], inc="sVE")
            ex = A("sc", lambda e, bk=bk: e.activation(
                expd[:, :], bk[:, 0:NCL], AF.Exp, bias=negm[:, :], scale=10.0),
                waits=pw + [nm10, t_h], inc="sSC")
            t_ex[rt % 2] = ex
            t_ex_last = ex
            sm = A("ve", lambda e: e.reduce_sum(scol[:, :], expd[:, :], axis=AX.X),
                   waits=[ex], inc="sVE")
            th = A("ve", lambda e: e.tensor_scalar_mul(thr[:, :], scol[:, :], 0.01),
                   waits=[sm], inc="sVE")
            ht = A("ve", lambda e: e.tensor_scalar(
                hits[:, :], expd[:, :], thr[:, :], None, ALU.is_ge),
                waits=[t_cnt, th], inc="sVE")
            t_cnt = A("te", lambda e, rt=rt: e.matmul(
                pb[4][0:1, 0:NCL], ones_f[:, :], hits[:, :],
                start=(rt == 0), stop=(rt == _DOTS_RT - 1), skip_group_check=True),
                waits=pw + [ht, t_ones], inc="sTE")
            t_h = A("ve", lambda e, rt=rt: e.reduce_sum(
                qcnt[:, rt:rt + 1], hits[:, :], axis=AX.X), waits=[ht], inc="sVE")
        if _DOTS_RT != NRT:
            A("ve", lambda e: e.tensor_copy(f32u[:, 352:416], pb[2][:, 0:NCL]),
              waits=[t_cnt, t_ex_last], inc="sVE")
        cv = A("ve", lambda e: e.tensor_copy(cntrow[:, :], pb[4][0:1, 0:NCL]),
               waits=[t_cnt], inc="sVE")
        cd = A("sc", lambda e: e.dma_start(out=cc_in[cc_idx], in_=cntrow[:, :]),
               waits=[cv], inc="sST")
        cc = A("gp", lambda e: e.collective_compute(
            "AllReduce", ALU.add, replica_groups=[list(range(NCORES))],
            ins=[cc_in[cc_idx].opt()], outs=[cc_out[cc_idx].opt()]),
            waits=pw + [cd], inc="sCC", amt=1)
        cl = A("sy", lambda e: e.dma_start(out=clrow[:, :], in_=cc_out[cc_idx]),
               waits=[cc], inc="sLD")
        cs = A("ve", lambda e: e.tensor_scalar(clrow[:, :], clrow[:, :], 0.5,
                                               None, ALU.is_ge),
               waits=[cl], inc="sVE")
        csd = A("sc", lambda e: e.dma_start(out=rowd[0:1, 7168:7232],
                                            in_=clrow[:, :]),
                waits=[cs], inc="sST")
        ccl = A("gp", lambda e: e.dma_start(
            out=clcol[:, :], in_=rowd[0:1, 7168:7232].rearrange("o c -> c o"),
            allow_slow_non_contiguous=True), waits=[csd], inc="sCA")
        q1 = A("ve", lambda e: e.tensor_scalar(
            qsel01[:, :], qcnt[:, :], 0.5, None, ALU.is_ge),
            waits=[t_h], inc="sVE")
        q2 = A("te", lambda e: e.transpose(pb[6][0:16, 0:128], qsel01[:, :],
                                           iden[:, :]),
               waits=pw + [q1, t_iden], inc="sTE")
        q3 = A("ve", lambda e: e.tensor_copy(qtr[:, :], pb[6][0:16, 0:128]),
               waits=[q2], inc="sVE")
        q4 = A("sc", lambda e: e.dma_start(
            out=rowd[0:1, 4096:6144].rearrange("o (t p) -> t p", p=128),
            in_=qtr[:, :]), waits=[q3], inc="sST")
        qselb = f32u[:, 4096:6144]
        q5 = A("sy", lambda e: e.dma_start(
            out=qselb[:, :], in_=rowd[0:1, 4096:6144].partition_broadcast(128)),
            waits=[q4], inc="sLD")
        return ccl, q5, qselb, cs

    def rowsel_mms(tok_ccl, tok_oh, pw):
        oh = bfu[0:64, bC:bC + 4096]
        mm = None
        for ot in range(NT):
            mm = A("te", lambda e, ot=ot: e.matmul(
                pb[5][:, ot:ot + 1], oh[:, ot * 128:(ot + 1) * 128], clcol[:, :],
                start=True, stop=True, skip_group_check=True),
                waits=pw + [tok_ccl, tok_oh],
                inc="sTE" if ot == NT - 1 else None)
        r1 = A("ve", lambda e: e.tensor_scalar_mul(rsel[:, :], pb[5][:, 0:NT], 0.5),
               waits=[mm], inc="sVE")
        r2 = A("ve", lambda e: e.tensor_mul(brsel[:, :], bias_s[:, :], rsel[:, :]),
               waits=[r1, t_bias], inc="sVE")
        return r2

    def dense_ffn(tok_h, tok_sel, tok_qselb, qselb, resid_src, dst, wsrc, pw):
        wbuf = [bfu[:, bD_:bD_ + 4096], bfu[:, bE:bE + 4096]]
        xst = [f32u[:, 0:2048], f32u[:, 2048:4096]]
        uev = [f32u[:, 6144:6656], f32u[:, 6656:7168]]
        xwr = [f32u[:, 7168:7680], f32u[:, 7680:8192]]
        t_w = [None, None]
        t_x = [None, None]
        t_wfree = [None, None]
        t_xfree = [None, None]
        t_ufree = [None, None]
        t_wrfree = [None, None]
        t_bfree = [None, None, None, None]
        ot_toks = [None] * NT
        st = None
        for ot in range(NT):
            ws = ot % 2
            t_w[ws] = A("gp", lambda e, ot=ot, ws=ws: e.dma_start(
                out=wbuf[ws][:, :], in_=wsrc[ot].rearrange("p t o -> p (t o)")),
                waits=pw + [t_wfree[ws]], inc="sCA")
            t_x[ws] = A("sy", lambda e, ot=ot, ws=ws: e.dma_start(
                out=xst[ws][:, :], in_=resid_src[ot * 128:(ot + 1) * 128, :]),
                waits=pw + [t_xfree[ws]], inc="sLD")
            for rc in range(4):
                i = ot * 4 + rc
                bs = i % 4
                us = i % 2
                bk = pb[bs]
                mm = None
                for t in range(NT):
                    mm = A("te", lambda e, bk=bk, t=t, rc=rc, ws=ws: e.matmul(
                        bk[:, :], wbuf[ws][:, t * 128:(t + 1) * 128],
                        big[:, t * R + rc * 512: t * R + rc * 512 + 512],
                        start=(t == 0), stop=(t == NT - 1), skip_group_check=True),
                        waits=pw + [tok_h[t], t_w[ws] if t == 0 else None,
                                    t_bfree[bs] if t == 0 else None],
                        inc="sTE" if t == NT - 1 else None)
                if rc == 3:
                    t_wfree[ws] = mm
                ev = A("sc", lambda e, bk=bk, us=us, ot=ot: e.activation(
                    uev[us][:, :], bk[:, :], AF.Identity,
                    bias=brsel[:, ot:ot + 1], scale=rsel[:, ot:ot + 1]),
                    waits=pw + [mm, tok_sel, t_ufree[us]], inc="sSC")
                t_bfree[bs] = ev
                m1 = A("ve", lambda e, us=us, rc=rc: e.tensor_mul(
                    uev[us][:, :], uev[us][:, :],
                    qselb[:, rc * 512:(rc + 1) * 512]),
                    waits=pw + [ev, tok_qselb], inc="sVE")
                m2 = A("ve", lambda e, us=us, rc=rc, ws=ws: e.tensor_add(
                    xwr[us][:, :], uev[us][:, :],
                    xst[ws][:, rc * 512:(rc + 1) * 512]),
                    waits=[m1, t_x[ws], t_wrfree[us]], inc="sVE")
                t_ufree[us] = m2
                if rc == 3:
                    t_xfree[ws] = m2
                st = A("sc", lambda e, us=us, ot=ot, rc=rc: e.dma_start(
                    out=dst[ot * 128:(ot + 1) * 128, rc * 512:(rc + 1) * 512],
                    in_=xwr[us][:, :]), waits=[m2], inc="sST")
                t_wrfree[us] = st
            ot_toks[ot] = st
        return ot_toks, st

    def qk_proj(tok_h, wsrc, dst, t_cos, t_sin, pw):
        cosb = f32u[:, 0:2048]
        sinb = f32u[:, 2048:4096]
        wbuf = [bfu[:, bD_:bD_ + 4096], bfu[:, bE:bE + 4096]]
        qraw = [f32u[:, 4096:6144], f32u[:, 6144:8192]]
        tmp = bfu[:, bC:bC + 4096].bitcast(F32)
        qout = [bfu[:, bA:bA + 2048], bfu[:, bA + 2048:bA + 4096]]
        t_w = [None, None]
        t_wfree = [None, None]
        t_qfree = [None, None]
        t_bfree = [None, None, None, None]
        t_ofree = [None, None]
        t_tfree = None
        last = None
        for u in range(NT):
            ws = u % 2
            qs = u % 2
            t_w[ws] = A("gp", lambda e, u=u, ws=ws: e.dma_start(
                out=wbuf[ws][:, :], in_=wsrc[u].rearrange("p t o -> p (t o)")),
                waits=pw + [t_wfree[ws]], inc="sCA")
            evs = []
            for rc in range(4):
                bs = rc
                bk = pb[bs]
                mm = None
                for t in range(NT):
                    mm = A("te", lambda e, bk=bk, t=t, rc=rc, ws=ws: e.matmul(
                        bk[:, :], wbuf[ws][:, t * 128:(t + 1) * 128],
                        big[:, t * R + rc * 512: t * R + rc * 512 + 512],
                        start=(t == 0), stop=(t == NT - 1), skip_group_check=True),
                        waits=pw + [tok_h[t], t_w[ws] if t == 0 else None,
                                    t_bfree[bs] if t == 0 else None],
                        inc="sTE" if t == NT - 1 else None)
                if rc == 3:
                    t_wfree[ws] = mm
                ev = A("sc", lambda e, bk=bk, qs=qs, rc=rc: e.activation(
                    qraw[qs][:, rc * 512:(rc + 1) * 512], bk[:, :], AF.Identity,
                    bias=0.0, scale=1.0),
                    waits=pw + [mm, t_qfree[qs] if rc == 0 else None], inc="sSC")
                t_bfree[bs] = ev
                evs.append(ev)
            s1 = A("sy", lambda e, qs=qs: e.dma_start(
                out=tmp[0:64, :], in_=qraw[qs][64:128, :]),
                waits=pw + evs + [t_tfree], inc="sLD")
            s2 = A("sy", lambda e, qs=qs: e.dma_start(
                out=tmp[64:128, :], in_=qraw[qs][0:64, :]), inc="sLD")
            r1 = A("ve", lambda e: e.tensor_mul(tmp[:, :], tmp[:, :], sinb[:, :]),
                   waits=pw + [s1, s2, t_sin], inc="sVE")
            r2 = A("ve", lambda e, qs=qs: e.tensor_mul(
                qraw[qs][:, :], qraw[qs][:, :], cosb[:, :]),
                waits=evs + [t_cos, s1, s2], inc="sVE")
            r3 = A("ve", lambda e, qs=qs: e.tensor_add(
                qout[qs][:, :], qraw[qs][:, :], tmp[:, :]),
                waits=[r1, r2, t_ofree[qs]], inc="sVE")
            t_qfree[qs] = r3
            t_tfree = r3
            last = A("sc", lambda e, u=u, qs=qs: e.dma_start(
                out=dst[u * 128:(u + 1) * 128, :], in_=qout[qs][:, :]),
                waits=[r3], inc="sST")
            t_ofree[qs] = last
        return last

    def v_proj(tok_h, pw):
        wvb = bfu[:, 0:16384]
        vev = [bfu[:, bF:bF + 512], bfu[:, bF + 512:bF + 1024]]
        t_bfree = [None, None, None, None]
        t_vfree = [None, None]
        last = None
        t_wprev = None
        for hc in range(8):
            t_w = A("gp", lambda e, hc=hc: e.dma_start(
                out=wvb[:, :], in_=wv_t[hc].rearrange("p t o -> p (t o)")),
                waits=pw + [t_wprev], inc="sCA")
            for rt in range(NRT):
                bs = rt % 4
                bk = pb[bs]
                mm = None
                for t in range(NT):
                    mm = A("te", lambda e, bk=bk, t=t, rt=rt: e.matmul(
                        bk[:, :], big[:, t * R + rt * 128: t * R + rt * 128 + 128],
                        wvb[:, t * 512:(t + 1) * 512],
                        start=(t == 0), stop=(t == NT - 1), skip_group_check=True),
                        waits=pw + [tok_h[t], t_w if t == 0 else None,
                                    t_bfree[bs] if t == 0 else None],
                        inc="sTE" if t == NT - 1 else None)
                if rt == NRT - 1:
                    t_wprev = mm
                ev = A("sc", lambda e, bk=bk, vs=rt % 2: e.activation(
                    vev[vs][:, :], bk[:, :], AF.Identity, bias=0.0, scale=1.0),
                    waits=pw + [mm, t_vfree[rt % 2]], inc="sSC")
                t_bfree[bs] = ev
                stv = A("sc", lambda e, rt=rt, hc=hc, vs=rt % 2: e.dma_start(
                    out=vd[rt * 128:(rt + 1) * 128, hc * 512:(hc + 1) * 512],
                    in_=vev[vs][:, :]), waits=[ev], inc="sST")
                t_vfree[rt % 2] = stv
                last = stv
        return last

    def attention(t_agk, t_agv, pw):
        kth = [bfu[:, bA:bA + 4096], bfu[:, bB:bB + 4096]]
        vh = [bfu[:, bC:bC + 4096], bfu[:, bD_:bD_ + 4096]]
        qth = [bfu[:, bE:bE + 2048], bfu[:, bE + 2048:bE + 4096]]
        expall = f32u[:, 2048:2816].bitcast(BF16)
        expb = [expall[:, 512 * i:512 * (i + 1)] for i in range(3)]
        rrow = f32u[0:1, 0:512]
        recipb = [f32u[:, 512:1024], f32u[:, 1024:1536]]
        dacc = f32u[:, 1536:2048]
        ctx_banks = [0, 1, 4]
        lbanks = [2, 3, 6]
        dn = pb[5]
        t_kvfree = [None, None]
        t_exp_pv = [None] * 3
        t_exp_da = [None] * 3
        t_ctxfree = [None] * 3
        t_dnfree = None
        t_rbfree = [None, None]
        t_rrowfree = None
        t_b7free = None
        t_exq = [None] * 3
        tok_ctx = {}
        kv_toks = {}
        gi = 0

        def load_head(h):
            hs = h % 2
            w = pw + [t_kvfree[hs], t_agk, t_agv]
            toks = []
            for r in range(2):
                toks.append(A("sy", lambda e, h=h, r=r, hs=hs: e.dma_start(
                    out=kth[hs].rearrange("p (i g c) -> p i g c", g=2, c=128)[:, :, r, :],
                    in_=kg[r * D + h * 128: r * D + (h + 1) * 128, :]
                        .rearrange("p (i c) -> p i c", c=128)),
                    waits=w, inc="sLD"))
                for sp in range(2):
                    toks.append(A("sy", lambda e, h=h, r=r, sp=sp, hs=hs: e.dma_start(
                        out=vh[hs].rearrange("p (s i g c) -> p s i g c",
                                             s=2, g=2, c=128)[:, sp, :, r, :],
                        in_=vg[r * R + sp * 1024: r * R + (sp + 1) * 1024,
                               h * 128:(h + 1) * 128]
                            .rearrange("(i p) c -> p i c", p=128)),
                        waits=w, inc="sLD"))
            toks.append(A("sy", lambda e, h=h, hs=hs: e.dma_start(
                out=qth[hs][:, :], in_=qTd[h * 128:(h + 1) * 128, :]),
                waits=w, inc="sLD"))
            kv_toks[h] = toks

        load_head(0)
        load_head(1)
        for h in range(H):
            hs = h % 2
            toks_h = kv_toks[h]
            last_pe = None
            for qc in range(4):
                nkb = 8 * qc + 8
                cb = ctx_banks[gi % 3]
                t_da = None
                srcs = [None] * nkb
                geom = []
                for kb in range(nkb):
                    j = kb - 8 * qc
                    if j < 0:
                        geom.append((0, 512))
                    else:
                        off = (j // 2) * 128
                        geom.append((off, 512 - off))

                def emit_front(kb):
                    nonlocal t_da
                    off, N = geom[kb]
                    j = kb - 8 * qc
                    es = kb % 3
                    lbk = pb[lbanks[es]]
                    mm = A("te", lambda e, lbk=lbk, kb=kb, qc=qc, off=off, N=N,
                           hs=hs: e.matmul(
                        lbk[:, off:off + N], kth[hs][:, kb * 128:(kb + 1) * 128],
                        qth[hs][:, qc * 512 + off: qc * 512 + off + N],
                        start=True, stop=True, skip_group_check=True),
                        waits=pw + toks_h + [t_exq[es]], inc="sTE")
                    ex = A("sc", lambda e, lbk=lbk, es=es, off=off, N=N: e.activation(
                        expb[es][:, off:off + N], lbk[:, off:off + N], AF.Exp,
                        bias=0.0, scale=ISQ),
                        waits=pw + [mm, t_exp_pv[es], t_exp_da[es]], inc="sSC")
                    t_exq[es] = ex
                    src = ex
                    if j >= 0:
                        m = j // 2
                        mi = j - 2 * m
                        src = A("gp", lambda e, es=es, m=m, mi=mi: e.tensor_mul(
                            expb[es][:, m * 128:(m + 1) * 128],
                            expb[es][:, m * 128:(m + 1) * 128],
                            mask2[:, mi * 128:(mi + 1) * 128]),
                            waits=pw + [ex, t_mask], inc="sGP")
                    if kb == 0:
                        t_da = A("ve", lambda e, es=es: e.tensor_copy(
                            dacc[:, :], expb[es][:, :]),
                            waits=pw + [src, t_dnfree], inc="sVE")
                    else:
                        t_da = A("ve", lambda e, es=es, off=off, N=N: e.tensor_add(
                            dacc[:, off:off + N], dacc[:, off:off + N],
                            expb[es][:, off:off + N]),
                            waits=[src, t_da], inc="sVE")
                    t_exp_da[es] = t_da
                    srcs[kb] = src

                for kb in range(min(3, nkb)):
                    emit_front(kb)
                for kb in range(nkb):
                    off, N = geom[kb]
                    es = kb % 3
                    pv = A("te", lambda e, es=es, off=off, N=N, kb=kb, nkb=nkb,
                           cb=cb, hs=hs: e.matmul(
                        pb[cb][:, off:off + N], vh[hs][:, kb * 128:(kb + 1) * 128],
                        expb[es][:, off:off + N],
                        start=(kb == 0), stop=(kb == nkb - 1),
                        skip_group_check=True),
                        waits=[srcs[kb], t_ctxfree[gi % 3] if kb == 0 else None],
                        inc="sTE")
                    t_exp_pv[es] = pv
                    last_pe = pv
                    if kb + 3 < nkb:
                        emit_front(kb + 3)
                dnm = A("te", lambda e: e.matmul(
                    dn[0:1, 0:512], ones_f[:, :], dacc[:, :],
                    start=True, stop=True, skip_group_check=True),
                    waits=pw + [t_da, t_ones, t_dnfree], inc="sTE")
                t_dnfree_mm = dnm
                rc1a = A("sc", lambda e: e.activation(rrow[:, :], dn[0:1, 0:512],
                                                      AF.Ln),
                         waits=pw + [dnm, t_rrowfree], inc="sSC")
                rc1 = A("sc", lambda e: e.activation(rrow[:, :], rrow[:, :],
                                                     AF.Exp, bias=0.0, scale=-1.0),
                        waits=[rc1a], inc="sSC")
                t_dnfree = rc1a
                rs = gi % 2
                rbm = A("te", lambda e: e.matmul(
                    pb[7][:, 0:512], ones_r[:, :], rrow[:, :],
                    start=True, stop=True, skip_group_check=True),
                    waits=pw + [rc1, t_onesr, t_b7free], inc="sTE")
                t_rrowfree = rbm
                rd2 = A("ve", lambda e, rs=rs: e.tensor_copy(
                    recipb[rs][:, :], pb[7][:, 0:512]),
                    waits=pw + [rbm, t_rbfree[rs]], inc="sVE")
                t_b7free = rd2
                nmt = A("ve", lambda e, cb=cb, h=h, qc=qc, rs=rs: e.tensor_mul(
                    big[:, h * R + qc * 512: h * R + (qc + 1) * 512],
                    pb[cb][:, :], recipb[rs][:, :]),
                    waits=pw + [rd2, last_pe], inc="sVE")
                t_rbfree[rs] = nmt
                t_ctxfree[gi % 3] = nmt
                tok_ctx[(h, qc)] = nmt
                gi += 1
            t_kvfree[hs] = last_pe
            if h + 2 < H:
                load_head(h + 2)
        return tok_ctx

    def wo_proj(tok_ctx, pw):
        wbuf = [bfu[:, bD_:bD_ + 4096], bfu[:, bE:bE + 4096]]
        x1rd = [f32u[:, 0:2048], f32u[:, 2048:4096]]
        xwr = [f32u[:, 6144:6656], f32u[:, 6656:7168]]
        t_w = [None, None]
        t_x = [None, None]
        t_wfree = [None, None]
        t_xfree = [None, None]
        t_wrfree = [None, None]
        t_bfree = [None, None, None, None]
        ot_toks = [None] * NT
        for ot in range(NT):
            ws = ot % 2
            t_w[ws] = A("gp", lambda e, ot=ot, ws=ws: e.dma_start(
                out=wbuf[ws][:, :], in_=wo_t[ot].rearrange("p t o -> p (t o)")),
                waits=pw + [t_wfree[ws]], inc="sCA")
            t_x[ws] = A("sy", lambda e, ot=ot, ws=ws: e.dma_start(
                out=x1rd[ws][:, :], in_=x1T[ot * 128:(ot + 1) * 128, :]),
                waits=pw + [t_xfree[ws]], inc="sLD")
            st = None
            for rc in range(4):
                i = ot * 4 + rc
                bs = i % 4
                us = i % 2
                bk = pb[bs]
                mm = None
                for t in range(NT):
                    mm = A("te", lambda e, bk=bk, t=t, rc=rc, ws=ws: e.matmul(
                        bk[:, :], wbuf[ws][:, t * 128:(t + 1) * 128],
                        big[:, t * R + rc * 512: t * R + rc * 512 + 512],
                        start=(t == 0), stop=(t == NT - 1), skip_group_check=True),
                        waits=pw + [tok_ctx[(t, rc)], t_w[ws] if t == 0 else None,
                                    t_bfree[bs] if t == 0 else None],
                        inc="sTE" if t == NT - 1 else None)
                if rc == 3:
                    t_wfree[ws] = mm
                m2 = A("ve", lambda e, bk=bk, us=us, rc=rc, ws=ws: e.tensor_add(
                    xwr[us][:, :], bk[:, :], x1rd[ws][:, rc * 512:(rc + 1) * 512]),
                    waits=pw + [mm, t_x[ws], t_wrfree[us]], inc="sVE")
                t_bfree[bs] = m2
                if rc == 3:
                    t_xfree[ws] = m2
                st = A("sc", lambda e, us=us, ot=ot, rc=rc: e.dma_start(
                    out=x2T[ot * 128:(ot + 1) * 128, rc * 512:(rc + 1) * 512],
                    in_=xwr[us][:, :]), waits=[m2], inc="sST")
                t_wrfree[us] = st
            ot_toks[ot] = st
        return ot_toks

    t_oh1 = build_onehot([])

    tok_h1 = ln_pass(x0T, None, [t_oh1, t_afd])
    pwa = bld.jall()
    tok_ccl1, tok_qselb1, qselb1, tok_cs1 = dots_sels(tok_h1, 0, pwa)
    tok_sel1 = rowsel_mms(tok_ccl1, t_oh1, pwa)
    pwd = bld.jall()
    if _STOP == "dots":
        dfu = nc.dram_tensor("dbg_f32u", [128, 8192], F32, kind="ExternalOutput").ap()
        t_f1 = A("sc", lambda e: e.dma_start(out=dfu, in_=f32u[:, :]),
                 waits=pwd, inc="sST")
        st1_toks = [t_f1] * NT
    else:
        st1_toks, t_f1 = dense_ffn(tok_h1, tok_sel1, tok_qselb1, qselb1,
                                   x0T, x1T, wf_t, pwd)

    if _STOP in ("ffn1", "dots"):
        t_f2 = t_f1
    pw = bld.jall()
    if _STOP in ("ffn1", "dots"):
        tok_h2 = None
    elif True:
        tok_h2 = ln_pass(x1T, st1_toks, pw)
    if _STOP not in ("ffn1", "dots"):
        pw2 = bld.jall()
        t_cos = A("sy", lambda e: e.dma_start(out=f32u[:, 0:2048], in_=cos_t),
                  waits=pw2, inc="sLD")
        t_sin = A("sy", lambda e: e.dma_start(out=f32u[:, 2048:4096], in_=sin_t),
                  waits=pw2, inc="sLD")
        t_q = qk_proj(tok_h2, wq_t, qTd, t_cos, t_sin, pw2)
        t_k = qk_proj(tok_h2, wk_t, kTd, t_cos, t_sin, [t_q])
        t_agk = A("gp", lambda e: e.collective_compute(
            "AllGather", ALU.bypass, replica_groups=[[0, 1], [2, 3], [4, 5], [6, 7]],
            ins=[kTd.opt()], outs=[kg.opt()]), waits=[t_k], inc="sCC", amt=1)
        pw3 = bld.jall()
        t_v = v_proj(tok_h2, pw3)
        if _STOP == "qkv":
            t_f2 = t_v
    if _STOP not in ("ffn1", "dots", "qkv"):
        t_agv = A("gp", lambda e: e.collective_compute(
            "AllGather", ALU.bypass, replica_groups=[[0, 1], [2, 3], [4, 5], [6, 7]],
            ins=[vd.opt()], outs=[vg.opt()]), waits=[t_v], inc="sCC", amt=1)

        pw4 = bld.jall()
        tok_ctx = attention(t_agk, t_agv, pw4)
        if _STOP == "attn":
            pwz2 = bld.jall()
            dctx = nc.dram_tensor("dbg_ctx", [128, NT * R], BF16,
                                  kind="ExternalOutput").ap()
            t_f2 = A("sc", lambda e: e.dma_start(out=dctx, in_=big[:, :]),
                     waits=pwz2, inc="sST")
    if _STOP not in ("ffn1", "dots", "qkv", "attn"):
        pw5 = bld.jall()
        st2_toks = wo_proj(tok_ctx, pw5)
        if _STOP == "wo":
            t_f2 = st2_toks[NT - 1]
    if _STOP not in ("ffn1", "dots", "qkv", "attn", "wo"):
        pw6 = bld.jall()
        t_oh2 = build_onehot(pw6)
        tok_h3 = ln_pass(x2T, st2_toks, pw6 + [t_oh2])
        pw7 = bld.jall()
        tok_ccl2, tok_qselb2, qselb2, tok_cs2 = dots_sels(tok_h3, 1, pw7)
        tok_sel2 = rowsel_mms(tok_ccl2, t_oh2, pw7)
        pw8 = bld.jall()
        _, t_f2 = dense_ffn(tok_h3, tok_sel2, tok_qselb2, qselb2,
                            x2T, outT, wf_t, pw8)

    t_dbg = None
    if _DBG:
        scr = {"x1T": (x1T, [D, R], F32), "x2T": (x2T, [D, R], F32),
               "qTd": (qTd, [D, R], BF16), "kTd": (kTd, [D, R], BF16),
               "vd": (vd, [R, D], BF16), "kg": (kg, [2 * D, R], BF16),
               "vg": (vg, [2 * R, D], BF16), "rowd": (rowd, [1, 8192], F32)}
        pwz = bld.jall()
        for nm in _DBG:
            if nm == "big":
                dbig = nc.dram_tensor("dbg_big", [128, NT * R], BF16,
                                      kind="ExternalOutput").ap()
                t_dbg = A("sc", lambda e, dbig=dbig: e.dma_start(
                    out=dbig, in_=big[:, :]), waits=pwz + [t_f2], inc="sST")
            else:
                ap, shp, dt_ = scr[nm]
                dd = nc.dram_tensor("dbg_" + nm, shp, dt_,
                                    kind="ExternalOutput").ap()
                t_dbg = A("sc", lambda e, ap=ap, dd=dd: e.dma_start(
                    out=dd, in_=ap), waits=pwz + [t_f2], inc="sST")

    A("sc", lambda e: None, waits=[t_f2, t_dbg])

    with nc.Block() as block:
        def run(engname, engobj):
            hi = {}
            for waits, fn, inc, amt in bld.q[engname]:
                for (s, v) in waits:
                    if hi.get(s, -1) >= v:
                        continue
                    engobj.wait_ge(handles[s], v)
                    hi[s] = v
                inst = fn(engobj)
                if inc is not None:
                    inst.then_inc(handles[inc], amt)

        @block.tensor
        def _(t):
            run("te", t)

        @block.vector
        def _(v):
            run("ve", v)

        @block.scalar
        def _(s_):
            run("sc", s_)

        @block.gpsimd
        def _(g):
            run("gp", g)

        @block.sync
        def _(s_):
            run("sy", s_)

    ctx.close()
    return nc



def _prep_shared(wq, wk, wv, wo, W, centroids, bias, assignments):
    c = np.ascontiguousarray
    wq_t = c(wq.reshape(NT, 128, NT, 128).transpose(2, 1, 0, 3))
    wk_t = c(wk.reshape(NT, 128, NT, 128).transpose(2, 1, 0, 3))
    wo_t = c(wo.reshape(NT, 128, NT, 128).transpose(2, 1, 0, 3))
    wf_t = c(W.T.reshape(NT, 128, NT, 128).transpose(2, 1, 0, 3))
    wv_t = c(wv.reshape(NT, 128, 8, 512).transpose(2, 1, 0, 3))
    ct_t = c(centroids.T.reshape(NT, 128, NCL).transpose(1, 0, 2))
    bias_t = c(bias.reshape(NT, 128).T)
    asn_t = np.ascontiguousarray(assignments[None, :].astype(np.int32))
    iden_t = np.eye(128, dtype=np.float32)
    return dict(wq_t=wq_t, wk_t=wk_t, wv_t=wv_t, wo_t=wo_t, wf_t=wf_t,
                ct_t=ct_t, bias_t=bias_t, asn_t=asn_t, iden_t=iden_t)


def _prep_half(half):
    inv = (1.0 / (10000.0 ** (np.arange(0, 128, 2, dtype=np.float64) / 128.0)))
    bi = np.arange(R) // 128
    pp = np.arange(R) % 128
    pos = (2 * bi + half) * 128 + pp
    ang = pos[None, :] * inv[(np.arange(128) % 64)][:, None]
    cos_t = np.cos(ang).astype(np.float32)
    sin = np.sin(ang)
    sgn = np.where(np.arange(128) < 64, -1.0, 1.0)
    sin_t = (sin * sgn[:, None]).astype(np.float32)
    tri = (np.arange(128)[:, None] <= np.arange(128)[None, :]).astype(np.float32)
    if half == 0:
        m0, m1 = tri, np.zeros((128, 128), np.float32)
    else:
        m0, m1 = np.ones((128, 128), np.float32), tri
    mask_d = np.concatenate([m0, m1], axis=1).astype(ml_dtypes.bfloat16)
    return dict(cos_t=cos_t, sin_t=sin_t, mask_d=mask_d)


_ROWIDX = {}


def _rows(half):
    if half not in _ROWIDX:
        idx = []
        for bi in range(16):
            j = 2 * bi + half
            idx.extend(range(j * 128, (j + 1) * 128))
        _ROWIDX[half] = np.asarray(idx)
    return _ROWIDX[half]


def kernel(L0_attn_hidden_states, L0_attn_wq, L0_attn_wk, L0_attn_wv,
           L0_attn_wo, L0_attn_kq1, L0_attn_kq2, L0_attn_kk1, L0_attn_kk2,
           L0_attn_scalingD, L0_attn_interaction_k, L0_attn_scalingD2,
           L0_ffn_weight, L0_ffn_bias, L0_ffn_centroids, L0_ffn_assignments):
    x = np.asarray(L0_attn_hidden_states, dtype=np.float32)
    shared = _prep_shared(np.asarray(L0_attn_wq, np.float32),
                          np.asarray(L0_attn_wk, np.float32),
                          np.asarray(L0_attn_wv, np.float32),
                          np.asarray(L0_attn_wo, np.float32),
                          np.asarray(L0_ffn_weight, np.float32),
                          np.asarray(L0_ffn_centroids, np.float32),
                          np.asarray(L0_ffn_bias, np.float32),
                          np.asarray(L0_ffn_assignments))
    halves = [_prep_half(0), _prep_half(1)]

    nc = _build()
    in_maps = []
    for c in range(NCORES):
        b, half = c // 2, c % 2
        x0T = np.ascontiguousarray(x[b][_rows(half), :].T)
        m = dict(shared)
        m.update(halves[half])
        m["x0T"] = x0T
        in_maps.append(m)

    res = run_bass_kernel_spmd(nc, in_maps, core_ids=list(range(NCORES)))

    out = np.empty((B, S, D), dtype=np.float32)
    for c in range(NCORES):
        b, half = c // 2, c % 2
        out[b][_rows(half), :] = np.asarray(res.results[c]["outT"]).T
    return out
